# version 09
# baseline (speedup 1.0000x reference)
import numpy as np
import ml_dtypes
import contextlib
import concourse.bass as bass
import concourse.mybir as mybir
from concourse.bass_utils import run_bass_kernel_spmd

F32 = mybir.dt.float32
BF16 = mybir.dt.bfloat16
AF = mybir.ActivationFunctionType
ALU = mybir.AluOpType

B = 8
L = 4096
D = 2048
NDT = D // 128
NST = 16
DTR = 64
PXP = 96
INTER = 8192
NIT = INTER // 128
T = 256
NCH = L // T
TP = 512
NPASS = L // TP
IGS = 8
NIG = NIT // IGS
LN_EPS = 1e-5

N_CTRL = 130156
NSAMP = 16777216
PERIOD = 1118481
CPERIOD = 8677
PACK = 387
NPACK = (PERIOD + PACK - 1) // PACK
NSLAB = (NPACK + 3) // 4
WIN = 7
ROWS = 21
SLABGRP = 16
NGRP = (NSLAB + SLABGRP - 1) // SLABGRP

_CONST_CACHE = {}
_NC_CACHE = {}


def _grid_consts():
    if "grid" in _CONST_CACHE:
        return _CONST_CACHE["grid"]
    t = np.linspace(0.0, N_CTRL - 1.0, NSAMP, dtype=np.float64)
    i_full = np.clip(np.floor(t).astype(np.int64), 0, N_CTRL - 2)
    f = (t[:PERIOD] - i_full[:PERIOD]).astype(np.float32)
    i0 = i_full[:PERIOD]
    del t, i_full
    v0 = np.arange(NPACK) * PACK
    i0_first = i0[v0].copy()
    f64 = f.astype(np.float64)
    f2 = f64 * f64
    f3 = f2 * f64
    coef = np.stack([
        0.5 * (-f64 + 2 * f2 - f3),
        0.5 * (2.0 - 5 * f2 + 3 * f3),
        0.5 * (f64 + 4 * f2 - 3 * f3),
        0.5 * (-f2 + f3),
    ], axis=0)
    pack_of = np.minimum(np.arange(PERIOD) // PACK, NPACK - 1)
    base = i0 - i0_first[pack_of]
    assert base.min() >= 0 and base.max() <= WIN - 4, (base.min(), base.max())
    Afull = np.zeros((WIN, PERIOD), dtype=np.float64)
    vs = np.arange(PERIOD)
    for k in range(4):
        np.add.at(Afull, (base + k, vs), coef[k])
    t0 = Afull.astype(ml_dtypes.bfloat16)
    r1 = Afull - t0.astype(np.float64)
    t1 = r1.astype(ml_dtypes.bfloat16)
    r2 = r1 - t1.astype(np.float64)
    t2 = r2.astype(ml_dtypes.bfloat16)
    terms = np.stack([t0, t1, t2], axis=0)
    del Afull, r1, r2, coef, f2, f3, f64
    padlen = NPACK * PACK
    tp = np.zeros((3, WIN, padlen), dtype=ml_dtypes.bfloat16)
    tp[:, :, :PERIOD] = terms
    tp = tp.reshape(3, WIN, NPACK, PACK)
    A = np.zeros((NSLAB, 128, PACK), dtype=ml_dtypes.bfloat16)
    npad4 = NSLAB * 4
    tq = np.zeros((3, WIN, npad4, PACK), dtype=ml_dtypes.bfloat16)
    tq[:, :, :NPACK] = tp
    tq = tq.reshape(3, WIN, NSLAB, 4, PACK)
    for q in range(4):
        for tm in range(3):
            for w in range(WIN):
                A[:, 32 * q + 7 * tm + w, :] = tq[tm, w, :, q, :]
    g = dict(A=np.ascontiguousarray(A), i0_first=i0_first)
    _CONST_CACHE["grid"] = g
    return g


def _stat_host(cp_g, cp_u, cp_d, i0_first):
    cps = [cp_g, cp_u, cp_d]
    stat = np.zeros((NSLAB, 128, 48), dtype=np.float32)
    w0 = i0_first - 1
    p_ids = np.arange(NSLAB)[:, None] * 4 + np.arange(4)[None, :]
    valid = p_ids < NPACK
    pv = np.where(valid, p_ids, 0)
    for mat, cp in enumerate(cps):
        for po in range(16):
            for w in range(WIN):
                idx = np.clip(CPERIOD * po + w0[pv] + w, 0, N_CTRL - 1)
                val = np.where(valid, cp[idx], 0.0).astype(np.float32)
                for q in range(4):
                    for tm in range(3):
                        stat[:, 32 * q + 7 * tm + w, mat * 16 + po] = val[:, q]
    return np.ascontiguousarray(stat.transpose(1, 0, 2).reshape(128, NSLAB * 48))


def _sel_const(v):
    s = np.zeros((16, 16 * 128), dtype=ml_dtypes.bfloat16)
    for n in range(16):
        s[n, n * 128:(n + 1) * 128] = v
    return s


def _layout_consts():
    if "small" in _CONST_CACHE:
        return _CONST_CACHE["small"]
    c = dict(
        ident_f32=np.eye(128, dtype=np.float32),
        ident_bf=np.eye(128).astype(ml_dtypes.bfloat16),
        ones_col_bf=np.ones((128, 1), dtype=ml_dtypes.bfloat16),
        ones_row_f32=np.ones((1, 128), dtype=np.float32),
        ones_row_bf=np.ones((1, 128), dtype=ml_dtypes.bfloat16),
        negones_row_bf=np.full((1, 128), -1.0, dtype=ml_dtypes.bfloat16),
        possel=_sel_const(1.0),
        negsel=_sel_const(-1.0),
    )
    _CONST_CACHE["small"] = c
    return c


class Ctr:
    def __init__(self, nc, st, name):
        self.sem = st.enter_context(nc.semaphore(name))
        self.n = 0

    def inc(self, ins, k=1):
        self.n += k
        ins.then_inc(self.sem, k)
        return self.n

    def dma(self, ins):
        return self.inc(ins, 16)


def build_program():
    nc = bass.Bass("TRN2", target_bir_lowering=False)
    x_h = nc.dram_tensor("x", [L, D], F32, kind="ExternalInput").ap()
    out_h = nc.dram_tensor("out", [L, D], F32, kind="ExternalOutput").ap()
    aconst_h = nc.dram_tensor("aconst", [NSLAB, 128, PACK], BF16, kind="ExternalInput").ap()
    stat_h = nc.dram_tensor("stat", [128, NSLAB * 48], F32, kind="ExternalInput").ap()
    wxT_h = nc.dram_tensor("wxT", [128, NDT * PXP], F32, kind="ExternalInput").ap()
    wdtT_h = nc.dram_tensor("wdtT", [DTR, D], F32, kind="ExternalInput").ap()
    bdt_h = nc.dram_tensor("bdt", [128, NDT], F32, kind="ExternalInput").ap()
    dcol_h = nc.dram_tensor("dcol", [128, NDT], F32, kind="ExternalInput").ap()
    alog_h = nc.dram_tensor("alog", [128, NDT * NST], F32, kind="ExternalInput").ap()
    identf_h = nc.dram_tensor("identf", [128, 128], F32, kind="ExternalInput").ap()
    identb_h = nc.dram_tensor("identb", [128, 128], BF16, kind="ExternalInput").ap()
    onescol_h = nc.dram_tensor("onescol", [128, 1], BF16, kind="ExternalInput").ap()
    onesrowf_h = nc.dram_tensor("onesrowf", [1, 128], F32, kind="ExternalInput").ap()
    onesrowb_h = nc.dram_tensor("onesrowb", [1, 128], BF16, kind="ExternalInput").ap()
    negonesrowb_h = nc.dram_tensor("negonesrowb", [1, 128], BF16, kind="ExternalInput").ap()
    possel_h = nc.dram_tensor("possel", [16, 16 * 128], BF16, kind="ExternalInput").ap()
    negsel_h = nc.dram_tensor("negsel", [16, 16 * 128], BF16, kind="ExternalInput").ap()
    wg_h = nc.dram_tensor("wg_scr", [INTER, D], BF16, kind="Internal").ap()
    wu_h = nc.dram_tensor("wu_scr", [INTER, D], BF16, kind="Internal").ap()
    wd_h = nc.dram_tensor("wd_scr", [D, INTER], BF16, kind="Internal").ap()
    wgT_h = nc.dram_tensor("wgT_scr", [D, INTER], BF16, kind="Internal").ap()
    wuT_h = nc.dram_tensor("wuT_scr", [D, INTER], BF16, kind="Internal").ap()
    wdT_h = nc.dram_tensor("wdT_scr", [INTER, D], BF16, kind="Internal").ap()
    wflat = {0: wg_h.rearrange("a b -> (a b)"), 1: wu_h.rearrange("a b -> (a b)"),
             2: wd_h.rearrange("a b -> (a b)")}

    st = contextlib.ExitStack()
    with st:
        cdx = Ctr(nc, st, "cdx")
        cdw = Ctr(nc, st, "cdw")
        cdc = Ctr(nc, st, "cdc")
        cds = Ctr(nc, st, "cds")
        crs = Ctr(nc, st, "crs")
        cT = Ctr(nc, st, "cT")
        cV = Ctr(nc, st, "cV")
        cS = Ctr(nc, st, "cS")
        cG = Ctr(nc, st, "cG")
        cVs = Ctr(nc, st, "cVs")
        cRA = Ctr(nc, st, "cRA")
        cRT = Ctr(nc, st, "cRT")
        cRE = Ctr(nc, st, "cRE")

        identf = st.enter_context(nc.sbuf_tensor("identf_sb", [128, 128], F32))
        identb = st.enter_context(nc.sbuf_tensor("identb_sb", [128, 128], BF16))
        onescol = st.enter_context(nc.sbuf_tensor("onescol_sb", [128, 1], BF16))
        onesrowf = st.enter_context(nc.sbuf_tensor("onesrowf_sb", [1, 128], F32))
        onesrowb = st.enter_context(nc.sbuf_tensor("onesrowb_sb", [1, 128], BF16))
        negonesrowb = st.enter_context(nc.sbuf_tensor("negonesrowb_sb", [1, 128], BF16))
        possel = st.enter_context(nc.sbuf_tensor("possel_sb", [16, 16 * 128], BF16))
        negsel = st.enter_context(nc.sbuf_tensor("negsel_sb", [16, 16 * 128], BF16))
        wxT = st.enter_context(nc.sbuf_tensor("wxT_sb", [128, NDT * PXP], BF16))
        wdtT = st.enter_context(nc.sbuf_tensor("wdtT_sb", [DTR, D], BF16))
        bdt = st.enter_context(nc.sbuf_tensor("bdt_sb", [128, NDT], F32))
        negb = st.enter_context(nc.sbuf_tensor("negb_sb", [128, NDT], F32))
        dcol = st.enter_context(nc.sbuf_tensor("dcol_sb", [128, NDT], F32))
        apos = st.enter_context(nc.sbuf_tensor("apos_sb", [128, NDT * NST], F32))
        state = st.enter_context(nc.sbuf_tensor("state_sb", [128, NDT * NST], F32))
        stats = st.enter_context(nc.sbuf_tensor("stats_sb", [1, 3 * TP], F32))

        banks = [st.enter_context(nc.psum_tensor(f"bank{i}", [128, 512], F32))
                 for i in range(8)]

        for dst, src in ((identf, identf_h), (identb, identb_h), (onescol, onescol_h),
                         (onesrowf, onesrowf_h), (onesrowb, onesrowb_h),
                         (negonesrowb, negonesrowb_h), (bdt, bdt_h), (dcol, dcol_h),
                         (apos, alog_h), (possel, possel_h), (negsel, negsel_h)):
            ins = nc.sync.dma_start(out=dst[:], in_=src)
            cdc.dma(ins)
        ins = nc.gpsimd.dma_start(out=wxT[:], in_=wxT_h)
        cdc.dma(ins)
        ins = nc.gpsimd.dma_start(out=wdtT[:], in_=wdtT_h)
        cdc.dma(ins)
        init_consts = cdc.n

        nc.vector.wait_ge(cdc.sem, init_consts)
        for _ in range(12):
            nc.vector.nop(cycle_cnt=500)
        nc.vector.memset(state[:], 0.0)
        ins = nc.vector.tensor_scalar_mul(negb[:], bdt[:], -1.0)
        cV.inc(ins)
        nc.scalar.wait_ge(cdc.sem, init_consts)
        for _ in range(12):
            nc.scalar.nop(cycle_cnt=500)
        ins = nc.scalar.activation(apos[:], apos[:], AF.Exp)
        cS.inc(ins)
        nc.vector.wait_ge(cS.sem, cS.n)
        ins = nc.vector.tensor_scalar_mul(apos[:], apos[:], -1.0)
        cV.inc(ins)

        grid = _grid_consts()
        with (
            nc.sbuf_tensor("stat_r", [128, NSLAB * 48], BF16) as stat_r,
            nc.sbuf_tensor("aring", [128, PACK * 32], BF16) as aring,
            nc.sbuf_tensor("stage", [48, SLABGRP * 4 * PACK], BF16) as stage,
        ):
            ARING = 32
            ins = nc.gpsimd.dma_start(out=stat_r[:], in_=stat_h)
            cdc.dma(ins)
            stat_ready = cdc.n
            av = aconst_h.rearrange("s p c -> p s c")
            for s0 in range(0, NSLAB, 8):
                slot = s0 % ARING
                n_s = min(8, NSLAB - s0)
                if s0 >= ARING:
                    nc.sync.wait_ge(cRT.sem, s0 - ARING + 8)
                ins = nc.sync.dma_start(
                    out=aring[:, slot * PACK:(slot + n_s) * PACK].rearrange(
                        "p (s c) -> p s c", s=n_s),
                    in_=av[:, s0:s0 + n_s, :])
                cRA.dma(ins)
                for j in range(n_s):
                    pass
            nc.tensor.wait_ge(cdc.sem, stat_ready)
            nc.vector.wait_ge(cdc.sem, stat_ready)
            nc.scalar.wait_ge(cdc.sem, stat_ready)
            for _ in range(12):
                nc.tensor.nop(cycle_cnt=500)
            ev_eng = [nc.vector, nc.scalar]
            for g_ in range(NGRP):
                s_lo = g_ * SLABGRP
                s_hi = min(s_lo + SLABGRP, NSLAB)
                for s_ in range(s_lo, s_hi):
                    slot = s_ % ARING
                    nc.tensor.wait_ge(cRA.sem, 16 * (s_ // 8 + 1))
                    if s_ % 8 == 0:
                        for _ in range(8):
                            nc.tensor.nop(cycle_cnt=500)
                    if s_ >= 2:
                        nc.tensor.wait_ge(cRE.sem, s_ - 1)
                    ins = None
                    for q in range(4):
                        if 4 * s_ + q >= NPACK:
                            break
                        bk = banks[(s_ % 2) * 4 + q]
                        ins = nc.tensor.matmul(
                            bk[0:48, 0:PACK],
                            stat_r[32 * q:32 * q + ROWS, s_ * 48:(s_ + 1) * 48],
                            aring[32 * q:32 * q + ROWS, slot * PACK:(slot + 1) * PACK],
                            start=True, stop=True, tile_position=(32 * q, 0))
                    cRT.inc(ins)
                    eng = ev_eng[s_ % 2]
                    eng.wait_ge(cRT.sem, cRT.n)
                    if g_ >= 1:
                        eng.wait_ge(crs.sem, stage_free_mark)
                    ins = None
                    for q in range(4):
                        if 4 * s_ + q >= NPACK:
                            break
                        bk = banks[(s_ % 2) * 4 + q]
                        col0 = ((s_ - s_lo) * 4 + q) * PACK
                        if eng is nc.scalar:
                            ins = eng.copy(stage[:, col0:col0 + PACK], bk[0:48, 0:PACK])
                        else:
                            ins = eng.tensor_copy(stage[:, col0:col0 + PACK], bk[0:48, 0:PACK])
                    cRE.inc(ins)
                v_lo = s_lo * 4 * PACK
                v_hi = min(s_hi * 4 * PACK, PERIOD)
                glen = v_hi - v_lo
                nc.gpsimd.wait_ge(cRE.sem, cRE.n)
                for mat in range(3):
                    for po in range(15):
                        row = mat * 16 + po
                        ins = nc.gpsimd.dma_start(
                            out=wflat[mat][po * PERIOD + v_lo: po * PERIOD + v_hi].unsqueeze(0),
                            in_=stage[row:row + 1, 0:glen])
                        crs.dma(ins)
                if g_ == 0:
                    for mat in range(3):
                        row = mat * 16 + 15
                        ins = nc.gpsimd.dma_start(
                            out=wflat[mat][NSAMP - 1: NSAMP].unsqueeze(0),
                            in_=stage[row:row + 1, 0:1])
                        crs.dma(ins)
                stage_free_mark = crs.n
            nc.sync.wait_ge(crs.sem, crs.n)
            nc.tensor.wait_ge(cRE.sem, cRE.n)
            with (
                nc.sbuf_tensor("tpin", [128, 6 * 2048], BF16) as tpin,
                nc.sbuf_tensor("tpasm", [128, 2 * 512], BF16) as tpasm,
            ):
                cTP = Ctr(nc, st, "cTP")
                cTT = Ctr(nc, st, "cTT")
                cTEv = Ctr(nc, st, "cTEv")
                cTEs = Ctr(nc, st, "cTEs")
                nblk = 0
                specs = [(wg_h, wgT_h, INTER, D), (wu_h, wuT_h, INTER, D),
                         (wd_h, wdT_h, D, INTER)]
                asm_marks = {}
                ld_marks = {}
                for mat, (wsrc, wdst, R, C) in enumerate(specs):
                    for cs in range(C // 2048):
                        for oq in range(R // 512):
                            for b_ in range(4):
                                slot = nblk % 6
                                if nblk >= 6:
                                    nc.sync.wait_ge(cTT.sem, 64 * ((nblk - 6) // 4 + 1))
                                ins = nc.sync.dma_start(
                                    out=tpin[:, slot * 2048:(slot + 1) * 2048],
                                    in_=wsrc[oq * 512 + b_ * 128: oq * 512 + (b_ + 1) * 128,
                                             cs * 2048:(cs + 1) * 2048])
                                cTP.dma(ins)
                                ld_marks[nblk] = cTP.n
                                nblk += 1
                            base = nblk - 4
                            base_ev = [cTEv.n, cTEs.n]
                            nc.tensor.wait_ge(cTEv.sem, base_ev[0])
                            nc.tensor.wait_ge(cTEs.sem, base_ev[1])
                            for ct in range(16):
                                aslot = ct % 2
                                for b_ in range(4):
                                    k = ct * 4 + b_
                                    bk = banks[k % 4]
                                    nc.tensor.wait_ge(cTP.sem, ld_marks[base + b_])
                                    if ct == 0:
                                        for _ in range(16):
                                            nc.tensor.nop(cycle_cnt=500)
                                    if k >= 4:
                                        kprev = k - 4
                                        ctr = cTEs if kprev % 2 else cTEv
                                        nc.tensor.wait_ge(ctr.sem,
                                                          base_ev[kprev % 2] + kprev // 2 + 1)
                                    ins = nc.tensor.matmul(
                                        bk[0:128, 0:128],
                                        tpin[:, ((base + b_) % 6) * 2048 + ct * 128:
                                             ((base + b_) % 6) * 2048 + (ct + 1) * 128],
                                        identb[:], start=True, stop=True)
                                    cTT.inc(ins)
                                tt_all = cTT.n
                                for b_ in range(4):
                                    k = ct * 4 + b_
                                    bk = banks[k % 4]
                                    eng = nc.scalar if k % 2 else nc.vector
                                    eng.wait_ge(cTT.sem, tt_all)
                                    if ct >= 2 and b_ < 2:
                                        eng.wait_ge(crs.sem, asm_marks.get((nblk, ct - 2), 0))
                                    if ct < 2 and b_ < 2:
                                        eng.wait_ge(crs.sem, crs.n)
                                    if eng is nc.scalar:
                                        ins = eng.copy(
                                            tpasm[:, aslot * 512 + b_ * 128: aslot * 512 + (b_ + 1) * 128],
                                            bk[0:128, 0:128])
                                        cTEs.inc(ins)
                                    else:
                                        ins = eng.tensor_copy(
                                            tpasm[:, aslot * 512 + b_ * 128: aslot * 512 + (b_ + 1) * 128],
                                            bk[0:128, 0:128])
                                        cTEv.inc(ins)
                                nc.sync.wait_ge(cTEv.sem, cTEv.n)
                                nc.sync.wait_ge(cTEs.sem, cTEs.n)
                                ins = nc.sync.dma_start(
                                    out=wdst[cs * 2048 + ct * 128: cs * 2048 + (ct + 1) * 128,
                                             oq * 512:(oq + 1) * 512],
                                    in_=tpasm[:, aslot * 512:(aslot + 1) * 512])
                                crs.dma(ins)
                                asm_marks[(nblk, ct)] = crs.n
        recon_stores = crs.n

        xin = st.enter_context(nc.sbuf_tensor("xin_sb", [128, 2 * D], F32))
        out1 = st.enter_context(nc.sbuf_tensor("out1_sb", [128, NDT * TP], F32))
        xn_ring = st.enter_context(nc.sbuf_tensor("xn_ring", [128, 2 * T], F32))
        xnbf = st.enter_context(nc.sbuf_tensor("xnbf_sb", [128, NDT * T], BF16))
        negd = st.enter_context(nc.sbuf_tensor("negd_sb", [128, 2 * T], BF16))
        sigt = st.enter_context(nc.sbuf_tensor("sigt_sb", [128, T], F32))
        ball = st.enter_context(nc.sbuf_tensor("ball_sb", [128, NST * T], BF16))
        call = st.enter_context(nc.sbuf_tensor("call_sb", [128, NST * (T + 1)], BF16))
        hall = st.enter_context(nc.sbuf_tensor("hall_sb", [128, NST * (T + 1)], F32))
        hcb = st.enter_context(nc.sbuf_tensor("hc_sb", [128, 2 * 8 * (T + 1)], BF16))
        oinit = st.enter_context(nc.sbuf_tensor("oinit_sb", [128, 1], F32))
        xpbf = st.enter_context(nc.sbuf_tensor("xpbf_sb", [DTR, T], BF16))
        xpB = st.enter_context(nc.sbuf_tensor("xpB_sb", [16, T], BF16))
        xpC = st.enter_context(nc.sbuf_tensor("xpC_sb", [16, T], BF16))
        ubf = st.enter_context(nc.sbuf_tensor("ubf_sb", [128, 2 * T], BF16))
        abar = st.enter_context(nc.sbuf_tensor("abar_sb", [128, 4 * (T + 1)], F32))
        bxr = st.enter_context(nc.sbuf_tensor("bxr_sb", [128, 4 * (T + 1)], BF16))
        xbf_ring = st.enter_context(nc.sbuf_tensor("xbf_ring", [128, 4 * T], BF16))
        xsq_ring = st.enter_context(nc.sbuf_tensor("xsq_ring", [128, 4 * T], BF16))
        tmp_sb = st.enter_context(nc.sbuf_tensor("tmp_sb", [128, TP], F32))
        sg_ring = st.enter_context(nc.sbuf_tensor("sg_ring", [128, 2 * TP], BF16))
        ub_ring = st.enter_context(nc.sbuf_tensor("ub_ring", [128, 2 * TP], BF16))
        xn2 = st.enter_context(nc.sbuf_tensor("xn2_sb", [128, NDT * TP], BF16))
        inter_g = st.enter_context(nc.sbuf_tensor("inter_sb", [128, IGS * TP], BF16))
        gwr = st.enter_context(nc.sbuf_tensor("gw_ring", [128, 2 * NDT * 256], BF16,
                                              is_transpose_dest=True))
        uwr = st.enter_context(nc.sbuf_tensor("uw_ring", [128, 2 * NDT * 256], BF16,
                                              is_transpose_dest=True))
        dwr = st.enter_context(nc.sbuf_tensor("dw_ring", [128, IGS * 256], BF16,
                                              is_transpose_dest=True))
        ostage = st.enter_context(nc.sbuf_tensor("ostage_sb", [128, D // 2], F32))

        for eng in (nc.tensor, nc.vector, nc.scalar, nc.gpsimd, nc.sync):
            eng.wait_ge(crs.sem, recon_stores)
        nc.vector.memset(oinit[:], 1.0)
        bx3 = bxr[:].rearrange("p (s t) -> p s t", s=4)
        ca3 = call[:].rearrange("p (n t) -> p n t", n=NST)
        nc.vector.memset(bx3[:, :, 0], 0.0)
        ins = nc.vector.memset(ca3[:, :, 0], 0.0)
        cV.inc(ins)

        def ewait(eng, ctr, val):
            eng.wait_ge(ctr.sem, val)

        xload_done = [0] * NCH
        tposed_done = [0] * NCH

        def emit_xload(c):
            t0 = c * T
            if c >= 1:
                nc.sync.wait_ge(cT.sem, tposed_done[c - 1])
            for tt in range(2):
                ins = nc.sync.dma_start(out=xin[:, tt * D:(tt + 1) * D],
                                        in_=x_h[t0 + tt * 128: t0 + (tt + 1) * 128, :])
                cdx.dma(ins)
            xload_done[c] = cdx.n

        def emit_mamba(c):
            cpar = c % 2
            ocol = lambda dt, t=0: dt * TP + cpar * T + t
            sum_marks = [0] * NDT
            ewait(nc.tensor, cdx, xload_done[c])
            if c == 0:
                for _ in range(8):
                    nc.tensor.nop(cycle_cnt=500)
            ewait(nc.tensor, cS, cS.n)
            ewait(nc.tensor, cV, cV.n)
            ev_marks = [0] * (2 * NDT)
            stg_marks = [0] * NDT
            for dt in range(NDT):
                rs = dt % 4
                for tt in range(2):
                    k = dt * 2 + tt
                    bk = banks[k % 2]
                    if k >= 2:
                        ewait(nc.tensor, cS, ev_marks[k - 2])
                    ins = nc.tensor.transpose(
                        bk[:, 0:128],
                        xin[:, tt * D + dt * 128: tt * D + (dt + 1) * 128],
                        identf[:])
                    cT.inc(ins)
                    ewait(nc.scalar, cT, cT.n)
                    ins = nc.scalar.copy(
                        out1[:, ocol(dt, tt * 128): ocol(dt, tt * 128) + 128],
                        bk[:, 0:128])
                    cS.inc(ins)
                    ev_marks[k] = cS.n
                if dt >= 4:
                    ewait(nc.scalar, cT, sum_marks[dt - 4])
                ins = nc.scalar.copy(xbf_ring[:, rs * T:(rs + 1) * T],
                                     out1[:, ocol(dt): ocol(dt) + T])
                cS.inc(ins)
                ins = nc.scalar.activation(xsq_ring[:, rs * T:(rs + 1) * T],
                                           out1[:, ocol(dt): ocol(dt) + T], AF.Square)
                cS.inc(ins)
                stg_marks[dt] = cS.n
                ewait(nc.tensor, cS, stg_marks[dt])
                nc.tensor.matmul(banks[2][0:1, 0:T], onescol[:],
                                 xbf_ring[:, rs * T:(rs + 1) * T],
                                 start=(dt == 0), stop=(dt == NDT - 1))
                ins = nc.tensor.matmul(banks[3][0:1, 0:T], onescol[:],
                                       xsq_ring[:, rs * T:(rs + 1) * T],
                                       start=(dt == 0), stop=(dt == NDT - 1))
                cT.inc(ins)
                sum_marks[dt] = cT.n
            tposed_done[c] = cT.n
            sums_done = cT.n
            sv = stats[:, 0:T]
            sr = stats[:, T:2 * T]
            sc = stats[:, 2 * T:3 * T]
            ewait(nc.vector, cT, sums_done)
            nc.vector.tensor_scalar_mul(sv, banks[2][0:1, 0:T], 1.0 / D)
            nc.vector.tensor_scalar_mul(sr, banks[3][0:1, 0:T], 1.0 / D)
            nc.vector.tensor_tensor(sc, sv, sv, op=ALU.mult)
            nc.vector.tensor_tensor(sr, sr, sc, op=ALU.subtract)
            ins = nc.vector.tensor_scalar_add(sr, sr, LN_EPS)
            cV.inc(ins)
            ewait(nc.scalar, cV, cV.n)
            ins = nc.scalar.activation(sr, sr, AF.Sqrt)
            cS.inc(ins)
            ewait(nc.vector, cS, cS.n)
            nc.vector.reciprocal(sr, sr)
            nc.vector.tensor_tensor(sc, sv, sr, op=ALU.mult)
            ins = nc.vector.tensor_scalar_mul(sc, sc, -1.0)
            cV.inc(ins)
            stats_done = cV.n
            ewait(nc.tensor, cV, stats_done)
            nc.tensor.matmul(banks[4][:, 0:T], onesrowf[:], sr, start=True, stop=True)
            ins = nc.tensor.matmul(banks[5][:, 0:T], onesrowf[:], sc, start=True, stop=True)
            cT.inc(ins)
            bcast_done = cT.n
            ewait(nc.vector, cT, bcast_done)
            for dt in range(NDT):
                xr = dt % 2
                xnv = xn_ring[:, xr * T:(xr + 1) * T]
                if dt >= 2:
                    ewait(nc.vector, cS, xnbf_marks[dt - 2])
                nc.vector.tensor_tensor(xnv, out1[:, ocol(dt): ocol(dt) + T],
                                        banks[4][:, 0:T], op=ALU.mult)
                ins = nc.vector.tensor_tensor(xnv, xnv, banks[5][:, 0:T], op=ALU.add)
                cV.inc(ins)
                ins = nc.vector.scalar_tensor_tensor(
                    out1[:, ocol(dt): ocol(dt) + T], xnv, dcol[:, dt:dt + 1],
                    out1[:, ocol(dt): ocol(dt) + T], op0=ALU.mult, op1=ALU.add)
                cV.inc(ins)
                ewait(nc.scalar, cV, cV.n)
                ins = nc.scalar.copy(xnbf[:, dt * T:(dt + 1) * T], xnv)
                cS.inc(ins)
                if dt == 0:
                    xnbf_marks = [0] * NDT
                xnbf_marks[dt] = cS.n
                ewait(nc.tensor, cS, cS.n)
                c0 = dt * PXP
                nc.tensor.matmul(banks[2][0:DTR, 0:T], wxT[:, c0:c0 + DTR],
                                 xnbf[:, dt * T:(dt + 1) * T],
                                 start=(dt == 0), stop=(dt == NDT - 1))
                nc.tensor.matmul(banks[3][0:16, 0:T], wxT[:, c0 + DTR:c0 + DTR + 16],
                                 xnbf[:, dt * T:(dt + 1) * T],
                                 start=(dt == 0), stop=(dt == NDT - 1))
                ins = nc.tensor.matmul(banks[6][0:16, 0:T], wxT[:, c0 + DTR + 16:c0 + PXP],
                                       xnbf[:, dt * T:(dt + 1) * T],
                                       start=(dt == 0), stop=(dt == NDT - 1))
            cT.inc(ins)
            xp_done = cT.n
            ewait(nc.scalar, cT, xp_done)
            nc.scalar.copy(xpbf[:], banks[2][0:DTR, 0:T])
            nc.scalar.copy(xpB[:], banks[3][0:16, 0:T])
            ins = nc.scalar.copy(xpC[:], banks[6][0:16, 0:T])
            cS.inc(ins)
            xpbf_done = cS.n
            ewait(nc.tensor, cS, xpbf_done)
            ewait(nc.tensor, cV, cV.n)
            if c >= 1:
                ewait(nc.tensor, cVs, c * NDT * NST)
                ewait(nc.tensor, cG, 2 * c * NDT)
            bcev_marks = [0] * NST
            for n in range(NST):
                bk = banks[6 + n % 2]
                if n >= 2:
                    ewait(nc.tensor, cS, bcev_marks[n - 2])
                nc.tensor.matmul(bk[:, 0:T], possel[:, n * 128:(n + 1) * 128],
                                 xpB[:], start=True, stop=True)
                ins = nc.tensor.matmul(bk[:, T:2 * T], possel[:, n * 128:(n + 1) * 128],
                                       xpC[:], start=True, stop=True)
                cT.inc(ins)
                ewait(nc.scalar, cT, cT.n)
                nc.scalar.copy(ball[:, n * T:(n + 1) * T], bk[:, 0:T])
                ins = nc.scalar.copy(call[:, n * (T + 1) + 1:(n + 1) * (T + 1)],
                                     bk[:, T:2 * T])
                cS.inc(ins)
                bcev_marks[n] = cS.n
            bc_done = cS.n
            TP1 = T + 1
            for dt in range(NDT):
                ndr = dt % 2
                dbk = banks[2 + dt % 2]
                if dt >= 2:
                    ewait(nc.tensor, cS, sig_marks[dt - 2])
                ins = nc.tensor.matmul(dbk[0:128, 0:T], wdtT[:, dt * 128:(dt + 1) * 128],
                                       xpbf[0:DTR, :], start=True, stop=True)
                cT.inc(ins)
                ewait(nc.scalar, cT, cT.n)
                ewait(nc.scalar, cVs, cVs.n)
                ewait(nc.scalar, cV, cV.n)
                nc.scalar.activation(sigt[:], dbk[0:128, 0:T], AF.Relu,
                                     bias=bdt[:, dt:dt + 1])
                nc.scalar.activation(xn_ring[:, 0:T], dbk[0:128, 0:T], AF.Abs,
                                     bias=bdt[:, dt:dt + 1])
                nc.scalar.activation(xn_ring[:, T:2 * T], xn_ring[:, 0:T], AF.Exp,
                                     scale=-1.0)
                ins = nc.scalar.activation(xn_ring[:, 0:T], xn_ring[:, T:2 * T], AF.Ln,
                                           bias=oinit[:, 0:1])
                cS.inc(ins)
                if dt == 0:
                    sig_marks = [0] * NDT
                sig_marks[dt] = cS.n
                ewait(nc.vector, cS, sig_marks[dt])
                ins = nc.vector.scalar_tensor_tensor(
                    negd[:, ndr * T:(ndr + 1) * T], sigt[:], oinit[:, 0:1], xn_ring[:, 0:T],
                    op0=ALU.mult, op1=ALU.add)
                cV.inc(ins)
                posd_mark = cV.n
                ewait(nc.scalar, cV, posd_mark)
                if c > 0 or dt > 0:
                    ewait(nc.vector, cG, cG.n)
                ins = nc.vector.tensor_tensor(ubf[:, ndr * T:(ndr + 1) * T],
                                              negd[:, ndr * T:(ndr + 1) * T],
                                              xnbf[:, dt * T:(dt + 1) * T], op=ALU.mult)
                cV.inc(ins)
                ewait(nc.vector, cS, bc_done)

                scan_marks = [0] * NST

                def emit_bx(k):
                    ar = k % 4
                    nc.vector.tensor_tensor(
                        bxr[:, ar * TP1 + 1:(ar + 1) * TP1],
                        ubf[:, ndr * T:(ndr + 1) * T],
                        ball[:, k * T:(k + 1) * T], op=ALU.mult)

                for n in range(NST):
                    ar = n % 4
                    g, r = divmod(n, 4)
                    if r == 0:
                        if g > 0:
                            ewait(nc.scalar, cVs, scan_marks[4 * g - 1])
                        a3 = abar[:].rearrange("p (s t) -> p s t", s=4)
                        nc.scalar.copy(a3[:, :, 0],
                                       state[:, dt * NST + 4 * g: dt * NST + 4 * g + 4])
                    av = abar[:, ar * TP1:(ar + 1) * TP1]
                    if n >= 4:
                        ewait(nc.scalar, cVs, scan_marks[n - 4])
                    ins = nc.scalar.activation(
                        av[:, 1:TP1], negd[:, ndr * T:(ndr + 1) * T], AF.Exp,
                        scale=apos[:, dt * NST + n: dt * NST + n + 1])
                    cS.inc(ins)
                    if n == 0:
                        emit_bx(0)
                        emit_bx(1)
                        emit_bx(2)
                    elif n + 2 < NST:
                        emit_bx(n + 2)
                    ewait(nc.vector, cS, cS.n)
                    ins = nc.vector.tensor_tensor_scan(
                        hall[:, n * TP1:(n + 1) * TP1], av,
                        bxr[:, ar * TP1:(ar + 1) * TP1],
                        oinit[:, 0:1], op0=ALU.mult, op1=ALU.add)
                    cVs.inc(ins)
                    scan_marks[n] = cVs.n
                    if n == 7 or n == 15:
                        half = n // 8
                        ewait(nc.gpsimd, cVs, cVs.n)
                        if c > 0 or dt > 0 or half > 0:
                            ewait(nc.gpsimd, cT,
                                  yhalf_marks.get((dt + NDT * c) * 2 + half - 2, 0))
                        ins3 = nc.gpsimd.tensor_tensor(
                            hcb[:, half * 8 * TP1:(half + 1) * 8 * TP1],
                            hall[:, half * 8 * TP1:(half + 1) * 8 * TP1],
                            call[:, half * 8 * TP1:(half + 1) * 8 * TP1], op=ALU.mult)
                        cG.inc(ins3)
                        ybk = banks[6 + dt % 2]
                        ewait(nc.tensor, cG, cG.n)
                        if half == 0 and dt >= 2:
                            ewait(nc.tensor, cV, yev_marks[dt - 2])
                        ins4 = None
                        for nn in range(8):
                            ins4 = nc.tensor.matmul(
                                ybk[:, 0:T], identb[:],
                                hcb[:, half * 8 * TP1 + nn * TP1 + 1:
                                    half * 8 * TP1 + (nn + 1) * TP1],
                                start=(half == 0 and nn == 0),
                                stop=(half == 1 and nn == 7))
                        cT.inc(ins4)
                        yhalf_marks[(dt + NDT * c) * 2 + half] = cT.n
                ewait(nc.scalar, cVs, scan_marks[NST - 1])
                h3 = hall[:].rearrange("p (n t) -> p n t", n=NST)
                ins = nc.scalar.copy(state[:, dt * NST:(dt + 1) * NST], h3[:, :, T])
                cS.inc(ins)
                ewait(nc.vector, cT, cT.n)
                ins = nc.vector.tensor_tensor(out1[:, ocol(dt): ocol(dt) + T],
                                              out1[:, ocol(dt): ocol(dt) + T],
                                              banks[6 + dt % 2][:, 0:T], op=ALU.add)
                cV.inc(ins)
                if dt == 0:
                    yev_marks = [0] * NDT
                yev_marks[dt] = cV.n

        yhalf_marks = {}

        gw_marks = {}
        dw_marks = {}

        def emit_wload_gu(p, iq):
            slot = iq % 2
            gkey = p * 32 + iq
            prev = gw_marks.get(gkey - 2)
            if prev:
                nc.sync.wait_ge(cT.sem, prev)
            gv = wgT_h.rearrange("(t p) c -> p t c", p=128)[:, :, iq * 256:(iq + 1) * 256]
            uv = wuT_h.rearrange("(t p) c -> p t c", p=128)[:, :, iq * 256:(iq + 1) * 256]
            do = gwr[:, slot * NDT * 256:(slot + 1) * NDT * 256].rearrange(
                "p (t c) -> p t c", t=NDT)
            uo = uwr[:, slot * NDT * 256:(slot + 1) * NDT * 256].rearrange(
                "p (t c) -> p t c", t=NDT)
            ins = nc.sync.dma_start(out=do, in_=gv)
            cdw.dma(ins)
            ins = nc.sync.dma_start(out=uo, in_=uv)
            cdw.dma(ins)
            return cdw.n

        def emit_wload_down(p, ig, hp):
            idx = p * NIG * 8 + ig * 8 + hp
            prev = dw_marks.get(idx - 1)
            if prev:
                nc.sync.wait_ge(cT.sem, prev)
            dv = wdT_h.rearrange("(t p) c -> p t c", p=128)[
                :, ig * IGS:(ig + 1) * IGS, hp * 256:(hp + 1) * 256]
            do = dwr[:].rearrange("p (t c) -> p t c", t=IGS)
            ins = nc.sync.dma_start(out=do, in_=dv)
            cdw.dma(ins)
            return cdw.n

        def emit_ffn(p):
            ln2_marks = [0] * NDT
            for dt in range(NDT):
                sl = dt % 2
                ocol0 = dt * TP
                if dt >= 2:
                    ewait(nc.scalar, cT, ln2_marks[dt - 2])
                ins = nc.scalar.copy(sg_ring[:, sl * TP: sl * TP + TP],
                                     out1[:, ocol0:ocol0 + TP])
                cS.inc(ins)
                ins = nc.scalar.activation(ub_ring[:, sl * TP: sl * TP + TP],
                                           out1[:, ocol0:ocol0 + TP], AF.Square)
                cS.inc(ins)
                ewait(nc.tensor, cS, cS.n)
                nc.tensor.matmul(banks[2][0:1, 0:TP], onescol[:],
                                 sg_ring[:, sl * TP: sl * TP + TP],
                                 start=(dt == 0), stop=(dt == NDT - 1))
                ins = nc.tensor.matmul(banks[3][0:1, 0:TP], onescol[:],
                                       ub_ring[:, sl * TP: sl * TP + TP],
                                       start=(dt == 0), stop=(dt == NDT - 1))
                cT.inc(ins)
                ln2_marks[dt] = cT.n
            sums_done = cT.n
            sv = stats[:, 0:TP]
            sr = stats[:, TP:2 * TP]
            sc = stats[:, 2 * TP:3 * TP]
            ewait(nc.vector, cT, sums_done)
            nc.vector.tensor_scalar_mul(sv, banks[2][0:1, 0:TP], 1.0 / D)
            nc.vector.tensor_scalar_mul(sr, banks[3][0:1, 0:TP], 1.0 / D)
            nc.vector.tensor_tensor(sc, sv, sv, op=ALU.mult)
            nc.vector.tensor_tensor(sr, sr, sc, op=ALU.subtract)
            ins = nc.vector.tensor_scalar_add(sr, sr, LN_EPS)
            cV.inc(ins)
            ewait(nc.scalar, cV, cV.n)
            ins = nc.scalar.activation(sr, sr, AF.Sqrt)
            cS.inc(ins)
            ewait(nc.vector, cS, cS.n)
            nc.vector.reciprocal(sr, sr)
            nc.vector.tensor_tensor(sc, sv, sr, op=ALU.mult)
            ins = nc.vector.tensor_scalar_mul(sc, sc, -1.0)
            cV.inc(ins)
            stats_done = cV.n
            ewait(nc.tensor, cV, stats_done)
            nc.tensor.matmul(banks[4][:, 0:TP], onesrowf[:], sr, start=True, stop=True)
            ins = nc.tensor.matmul(banks[5][:, 0:TP], onesrowf[:], sc, start=True, stop=True)
            cT.inc(ins)
            bcast_done = cT.n
            ewait(nc.vector, cT, bcast_done)
            for dt in range(NDT):
                nc.vector.tensor_tensor(tmp_sb[:], out1[:, dt * TP:(dt + 1) * TP],
                                        banks[4][:, 0:TP], op=ALU.mult)
                ins = nc.vector.tensor_tensor(xn2[:, dt * TP:(dt + 1) * TP], tmp_sb[:],
                                              banks[5][:, 0:TP], op=ALU.add)
                cV.inc(ins)
            xn2_done = cV.n

            for ig in range(NIG):
                for ii in range(IGS):
                    i = ig * IGS + ii
                    iq, ip = divmod(i, 2)
                    slot = iq % 2
                    if ip == 0:
                        gw_done = emit_wload_gu(p, iq)
                    gbk = banks[i % 2]
                    ubk = banks[2 + i % 2]
                    ewait(nc.tensor, cdw, gw_done)
                    if ip == 0:
                        for _ in range(8):
                            nc.tensor.nop(cycle_cnt=500)
                    if i >= 2:
                        ewait(nc.tensor, cS, silu_marks[i - 2])
                    ewait(nc.tensor, cV, xn2_done)
                    for dt in range(NDT):
                        nc.tensor.matmul(gbk[:, 0:TP],
                                         gwr[:, slot * NDT * 256 + dt * 256 + ip * 128:
                                             slot * NDT * 256 + dt * 256 + ip * 128 + 128],
                                         xn2[:, dt * TP:(dt + 1) * TP],
                                         start=(dt == 0), stop=(dt == NDT - 1))
                    for dt in range(NDT):
                        ins = nc.tensor.matmul(ubk[:, 0:TP],
                                               uwr[:, slot * NDT * 256 + dt * 256 + ip * 128:
                                                   slot * NDT * 256 + dt * 256 + ip * 128 + 128],
                                               xn2[:, dt * TP:(dt + 1) * TP],
                                               start=(dt == 0), stop=(dt == NDT - 1))
                    cT.inc(ins)
                    if ip == 1:
                        gw_marks[p * 32 + iq] = cT.n
                    mm_done = cT.n
                    ewait(nc.scalar, cT, mm_done)
                    nc.scalar.activation(sg_ring[:, (i % 2) * TP:(i % 2) * TP + TP],
                                         gbk[:, 0:TP], AF.Silu)
                    ins = nc.scalar.copy(ub_ring[:, (i % 2) * TP:(i % 2) * TP + TP],
                                         ubk[:, 0:TP])
                    cS.inc(ins)
                    if i == 0:
                        silu_marks = [0] * NIT
                    silu_marks[i] = cS.n
                    ewait(nc.vector, cS, silu_marks[i])
                    ins = nc.vector.tensor_tensor(inter_g[:, ii * TP:(ii + 1) * TP],
                                                  sg_ring[:, (i % 2) * TP:(i % 2) * TP + TP],
                                                  ub_ring[:, (i % 2) * TP:(i % 2) * TP + TP],
                                                  op=ALU.mult)
                    cV.inc(ins)
                inter_done = cV.n
                for hp_ in range(8):
                    dw_done = emit_wload_down(p, ig, hp_)
                    for hh in range(2):
                        h = hp_ * 2 + hh
                        dbk = banks[6 + h % 2]
                        ewait(nc.tensor, cdw, dw_done)
                        if hh == 0:
                            for _ in range(8):
                                nc.tensor.nop(cycle_cnt=500)
                        if h == 0:
                            dnev_marks = [0] * 16
                            ewait(nc.tensor, cV, inter_done)
                        elif h >= 2:
                            ewait(nc.tensor, cV, dnev_marks[h - 2])
                        ins = None
                        for ii in range(IGS):
                            ins = nc.tensor.matmul(
                                dbk[:, 0:TP],
                                dwr[:, ii * 256 + hh * 128: ii * 256 + hh * 128 + 128],
                                inter_g[:, ii * TP:(ii + 1) * TP],
                                start=(ii == 0), stop=(ii == IGS - 1))
                        cT.inc(ins)
                        if hh == 1:
                            dw_marks[p * NIG * 8 + ig * 8 + hp_] = cT.n
                        ewait(nc.vector, cT, cT.n)
                        ins = nc.vector.tensor_tensor(out1[:, h * TP:(h + 1) * TP],
                                                      out1[:, h * TP:(h + 1) * TP],
                                                      dbk[:, 0:TP], op=ALU.add)
                        cV.inc(ins)
                        dnev_marks[h] = cV.n
            final_done = cV.n
            t0 = p * TP
            ewait(nc.tensor, cV, final_done)
            fin_marks = [0] * (4 * NDT)
            for tt in range(4):
                for dh in range(2):
                    if tt + dh >= 1:
                        ewait(nc.scalar, cds, cds.n)
                    for dt2 in range(NDT // 2):
                        dt = dh * (NDT // 2) + dt2
                        bk = banks[dt % 2]
                        kk = tt * NDT + dh * (NDT // 2) + dt2
                        if kk >= 2:
                            ewait(nc.tensor, cS, fin_marks[kk - 2])
                        ins = nc.tensor.transpose(
                            bk[:, 0:128],
                            out1[:, dt * TP + tt * 128: dt * TP + tt * 128 + 128],
                            identf[:])
                        cT.inc(ins)
                        ewait(nc.scalar, cT, cT.n)
                        ins = nc.scalar.copy(ostage[:, dt2 * 128:(dt2 + 1) * 128],
                                             bk[:, 0:128])
                        cS.inc(ins)
                        fin_marks[kk] = cS.n
                    ewait(nc.sync, cS, cS.n)
                    ins = nc.sync.dma_start(
                        out=out_h[t0 + tt * 128: t0 + (tt + 1) * 128,
                                  dh * (D // 2):(dh + 1) * (D // 2)],
                        in_=ostage[:])
                    cds.dma(ins)

        emit_xload(0)
        for p in range(NPASS):
            for half in range(2):
                c = 2 * p + half
                emit_mamba(c)
                if c + 1 < NCH:
                    emit_xload(c + 1)
            emit_ffn(p)

        nc.sync.wait_ge(cds.sem, cds.n)
        nc.gpsimd.wait_ge(cds.sem, cds.n)
    return nc


def _prep_inputs(inputs):
    x = np.ascontiguousarray(inputs["L0_attn_x"])
    A_log = inputs["L0_attn_A_log"]
    Dv = inputs["L0_attn_D"]
    W_x = inputs["L0_attn_W_x"]
    W_dt = inputs["L0_attn_W_dt"]
    b_dt = inputs["L0_attn_b_dt"]
    grid = _grid_consts()
    small = _layout_consts()
    stat = _stat_host(inputs["L0_ffn_gate_cp"], inputs["L0_ffn_up_cp"],
                      inputs["L0_ffn_down_cp"], grid["i0_first"])
    wxT = np.ascontiguousarray(
        W_x.T.reshape(NDT, 128, PXP).transpose(1, 0, 2).reshape(128, NDT * PXP))
    wdtT = np.ascontiguousarray(W_dt.T)
    bdt = np.ascontiguousarray(b_dt.reshape(NDT, 128).T)
    dcol = np.ascontiguousarray(Dv.reshape(NDT, 128).T)
    alog = np.ascontiguousarray(
        A_log.reshape(NDT, 128, NST).transpose(1, 0, 2).reshape(128, NDT * NST))
    shared = dict(
        aconst=grid["A"], stat=stat, wxT=wxT, wdtT=wdtT, bdt=bdt, dcol=dcol,
        alog=alog, identf=small["ident_f32"], identb=small["ident_bf"],
        onescol=small["ones_col_bf"], onesrowf=small["ones_row_f32"],
        onesrowb=small["ones_row_bf"], negonesrowb=small["negones_row_bf"],
        possel=small["possel"], negsel=small["negsel"],
    )
    in_maps = []
    for b in range(B):
        m = dict(shared)
        m["x"] = np.ascontiguousarray(x[b])
        in_maps.append(m)
    return in_maps


def kernel(**inputs):
    in_maps = _prep_inputs(inputs)
    if "nc" not in _NC_CACHE:
        _NC_CACHE["nc"] = build_program()
    nc = _NC_CACHE["nc"]
    res = run_bass_kernel_spmd(nc, in_maps, core_ids=list(range(B)))
    out = np.stack([r["out"] for r in res.results], axis=0)
    return out.astype(np.float32)


# version 11
# speedup vs baseline: 1.0482x; 1.0482x over previous
import numpy as np
import ml_dtypes
import contextlib
import concourse.bass as bass
import concourse.mybir as mybir
from concourse.bass_utils import run_bass_kernel_spmd

F32 = mybir.dt.float32
BF16 = mybir.dt.bfloat16
AF = mybir.ActivationFunctionType
ALU = mybir.AluOpType

B = 8
L = 4096
D = 2048
NDT = D // 128
NST = 16
DTR = 64
PXP = 96
INTER = 8192
NIT = INTER // 128
T = 256
NCH = L // T
TP = 512
NPASS = L // TP
IGS = 8
NIG = NIT // IGS
LN_EPS = 1e-5

N_CTRL = 130156
NSAMP = 16777216
PERIOD = 1118481
CPERIOD = 8677
PACK = 387
NPACK = (PERIOD + PACK - 1) // PACK
NSLAB = (NPACK + 3) // 4
WIN = 7
ROWS = 21
SLABGRP = 16
NGRP = (NSLAB + SLABGRP - 1) // SLABGRP

_CONST_CACHE = {}
_NC_CACHE = {}


def _grid_consts():
    if "grid" in _CONST_CACHE:
        return _CONST_CACHE["grid"]
    t = np.linspace(0.0, N_CTRL - 1.0, NSAMP, dtype=np.float64)
    i_full = np.clip(np.floor(t).astype(np.int64), 0, N_CTRL - 2)
    f = (t[:PERIOD] - i_full[:PERIOD]).astype(np.float32)
    i0 = i_full[:PERIOD]
    del t, i_full
    v0 = np.arange(NPACK) * PACK
    i0_first = i0[v0].copy()
    f64 = f.astype(np.float64)
    f2 = f64 * f64
    f3 = f2 * f64
    coef = np.stack([
        0.5 * (-f64 + 2 * f2 - f3),
        0.5 * (2.0 - 5 * f2 + 3 * f3),
        0.5 * (f64 + 4 * f2 - 3 * f3),
        0.5 * (-f2 + f3),
    ], axis=0)
    pack_of = np.minimum(np.arange(PERIOD) // PACK, NPACK - 1)
    base = i0 - i0_first[pack_of]
    assert base.min() >= 0 and base.max() <= WIN - 4, (base.min(), base.max())
    Afull = np.zeros((WIN, PERIOD), dtype=np.float64)
    vs = np.arange(PERIOD)
    for k in range(4):
        np.add.at(Afull, (base + k, vs), coef[k])
    t0 = Afull.astype(ml_dtypes.bfloat16)
    r1 = Afull - t0.astype(np.float64)
    t1 = r1.astype(ml_dtypes.bfloat16)
    r2 = r1 - t1.astype(np.float64)
    t2 = r2.astype(ml_dtypes.bfloat16)
    terms = np.stack([t0, t1, t2], axis=0)
    del Afull, r1, r2, coef, f2, f3, f64
    padlen = NPACK * PACK
    tp = np.zeros((3, WIN, padlen), dtype=ml_dtypes.bfloat16)
    tp[:, :, :PERIOD] = terms
    tp = tp.reshape(3, WIN, NPACK, PACK)
    A = np.zeros((NSLAB, 128, PACK), dtype=ml_dtypes.bfloat16)
    npad4 = NSLAB * 4
    tq = np.zeros((3, WIN, npad4, PACK), dtype=ml_dtypes.bfloat16)
    tq[:, :, :NPACK] = tp
    tq = tq.reshape(3, WIN, NSLAB, 4, PACK)
    for q in range(4):
        for tm in range(3):
            for w in range(WIN):
                A[:, 32 * q + 7 * tm + w, :] = tq[tm, w, :, q, :]
    g = dict(A=np.ascontiguousarray(A), i0_first=i0_first)
    _CONST_CACHE["grid"] = g
    return g


def _stat_host(cp_g, cp_u, cp_d, i0_first):
    cps = [cp_g, cp_u, cp_d]
    stat = np.zeros((NSLAB, 128, 48), dtype=np.float32)
    w0 = i0_first - 1
    p_ids = np.arange(NSLAB)[:, None] * 4 + np.arange(4)[None, :]
    valid = p_ids < NPACK
    pv = np.where(valid, p_ids, 0)
    for mat, cp in enumerate(cps):
        for po in range(16):
            for w in range(WIN):
                idx = np.clip(CPERIOD * po + w0[pv] + w, 0, N_CTRL - 1)
                val = np.where(valid, cp[idx], 0.0).astype(np.float32)
                for q in range(4):
                    for tm in range(3):
                        stat[:, 32 * q + 7 * tm + w, mat * 16 + po] = val[:, q]
    return np.ascontiguousarray(stat.transpose(1, 0, 2).reshape(128, NSLAB * 48))


def _sel_const(v):
    s = np.zeros((16, 16 * 128), dtype=ml_dtypes.bfloat16)
    for n in range(16):
        s[n, n * 128:(n + 1) * 128] = v
    return s


def _layout_consts():
    if "small" in _CONST_CACHE:
        return _CONST_CACHE["small"]
    c = dict(
        ident_f32=np.eye(128, dtype=np.float32),
        ident_bf=np.eye(128).astype(ml_dtypes.bfloat16),
        ones_col_bf=np.ones((128, 1), dtype=ml_dtypes.bfloat16),
        ones_row_f32=np.ones((1, 128), dtype=np.float32),
        ones_row_bf=np.ones((1, 128), dtype=ml_dtypes.bfloat16),
        negones_row_bf=np.full((1, 128), -1.0, dtype=ml_dtypes.bfloat16),
        possel=_sel_const(1.0),
        negsel=_sel_const(-1.0),
    )
    _CONST_CACHE["small"] = c
    return c


class Ctr:
    def __init__(self, nc, st, name):
        self.sem = st.enter_context(nc.semaphore(name))
        self.n = 0

    def inc(self, ins, k=1):
        self.n += k
        ins.then_inc(self.sem, k)
        return self.n

    def dma(self, ins):
        return self.inc(ins, 16)


def build_program():
    nc = bass.Bass("TRN2", target_bir_lowering=False)
    x_h = nc.dram_tensor("x", [L, D], F32, kind="ExternalInput").ap()
    out_h = nc.dram_tensor("out", [L, D], F32, kind="ExternalOutput").ap()
    aconst_h = nc.dram_tensor("aconst", [NSLAB, 128, PACK], BF16, kind="ExternalInput").ap()
    stat_h = nc.dram_tensor("stat", [128, NSLAB * 48], F32, kind="ExternalInput").ap()
    wxT_h = nc.dram_tensor("wxT", [128, NDT * PXP], F32, kind="ExternalInput").ap()
    wdtT_h = nc.dram_tensor("wdtT", [DTR, D], F32, kind="ExternalInput").ap()
    bdt_h = nc.dram_tensor("bdt", [128, NDT], F32, kind="ExternalInput").ap()
    dcol_h = nc.dram_tensor("dcol", [128, NDT], F32, kind="ExternalInput").ap()
    alog_h = nc.dram_tensor("alog", [128, NDT * NST], F32, kind="ExternalInput").ap()
    identf_h = nc.dram_tensor("identf", [128, 128], F32, kind="ExternalInput").ap()
    identb_h = nc.dram_tensor("identb", [128, 128], BF16, kind="ExternalInput").ap()
    onescol_h = nc.dram_tensor("onescol", [128, 1], BF16, kind="ExternalInput").ap()
    onesrowf_h = nc.dram_tensor("onesrowf", [1, 128], F32, kind="ExternalInput").ap()
    onesrowb_h = nc.dram_tensor("onesrowb", [1, 128], BF16, kind="ExternalInput").ap()
    negonesrowb_h = nc.dram_tensor("negonesrowb", [1, 128], BF16, kind="ExternalInput").ap()
    possel_h = nc.dram_tensor("possel", [16, 16 * 128], BF16, kind="ExternalInput").ap()
    negsel_h = nc.dram_tensor("negsel", [16, 16 * 128], BF16, kind="ExternalInput").ap()
    wg_h = nc.dram_tensor("wg_scr", [INTER, D], BF16, kind="Internal").ap()
    wu_h = nc.dram_tensor("wu_scr", [INTER, D], BF16, kind="Internal").ap()
    wd_h = nc.dram_tensor("wd_scr", [D, INTER], BF16, kind="Internal").ap()
    wgT_h = nc.dram_tensor("wgT_scr", [D, INTER], BF16, kind="Internal").ap()
    wuT_h = nc.dram_tensor("wuT_scr", [D, INTER], BF16, kind="Internal").ap()
    wdT_h = nc.dram_tensor("wdT_scr", [INTER, D], BF16, kind="Internal").ap()
    wflat = {0: wg_h.rearrange("a b -> (a b)"), 1: wu_h.rearrange("a b -> (a b)"),
             2: wd_h.rearrange("a b -> (a b)")}

    st = contextlib.ExitStack()
    with st:
        cdx = Ctr(nc, st, "cdx")
        cdw = Ctr(nc, st, "cdw")
        cdc = Ctr(nc, st, "cdc")
        cds = Ctr(nc, st, "cds")
        crs = Ctr(nc, st, "crs")
        cT = Ctr(nc, st, "cT")
        cV = Ctr(nc, st, "cV")
        cS = Ctr(nc, st, "cS")
        cG = Ctr(nc, st, "cG")
        cVs = Ctr(nc, st, "cVs")
        cRA = Ctr(nc, st, "cRA")
        cRT = Ctr(nc, st, "cRT")
        cRE = Ctr(nc, st, "cRE")

        identf = st.enter_context(nc.sbuf_tensor("identf_sb", [128, 128], F32))
        identb = st.enter_context(nc.sbuf_tensor("identb_sb", [128, 128], BF16))
        onescol = st.enter_context(nc.sbuf_tensor("onescol_sb", [128, 1], BF16))
        onesrowf = st.enter_context(nc.sbuf_tensor("onesrowf_sb", [1, 128], F32))
        onesrowb = st.enter_context(nc.sbuf_tensor("onesrowb_sb", [1, 128], BF16))
        negonesrowb = st.enter_context(nc.sbuf_tensor("negonesrowb_sb", [1, 128], BF16))
        possel = st.enter_context(nc.sbuf_tensor("possel_sb", [16, 16 * 128], BF16))
        negsel = st.enter_context(nc.sbuf_tensor("negsel_sb", [16, 16 * 128], BF16))
        wxT = st.enter_context(nc.sbuf_tensor("wxT_sb", [128, NDT * PXP], BF16))
        wdtT = st.enter_context(nc.sbuf_tensor("wdtT_sb", [DTR, D], BF16))
        bdt = st.enter_context(nc.sbuf_tensor("bdt_sb", [128, NDT], F32))
        negb = st.enter_context(nc.sbuf_tensor("negb_sb", [128, NDT], F32))
        dcol = st.enter_context(nc.sbuf_tensor("dcol_sb", [128, NDT], F32))
        apos = st.enter_context(nc.sbuf_tensor("apos_sb", [128, NDT * NST], F32))
        state = st.enter_context(nc.sbuf_tensor("state_sb", [128, NDT * NST], F32))
        stats = st.enter_context(nc.sbuf_tensor("stats_sb", [1, 3 * TP], F32))

        banks = [st.enter_context(nc.psum_tensor(f"bank{i}", [128, 512], F32))
                 for i in range(8)]

        for dst, src in ((identf, identf_h), (identb, identb_h), (onescol, onescol_h),
                         (onesrowf, onesrowf_h), (onesrowb, onesrowb_h),
                         (negonesrowb, negonesrowb_h), (bdt, bdt_h), (dcol, dcol_h),
                         (apos, alog_h), (possel, possel_h), (negsel, negsel_h)):
            ins = nc.sync.dma_start(out=dst[:], in_=src)
            cdc.dma(ins)
        ins = nc.gpsimd.dma_start(out=wxT[:], in_=wxT_h)
        cdc.dma(ins)
        ins = nc.gpsimd.dma_start(out=wdtT[:], in_=wdtT_h)
        cdc.dma(ins)
        init_consts = cdc.n

        nc.vector.wait_ge(cdc.sem, init_consts)
        for _ in range(12):
            nc.vector.nop(cycle_cnt=500)
        nc.vector.memset(state[:], 0.0)
        ins = nc.vector.tensor_scalar_mul(negb[:], bdt[:], -1.0)
        cV.inc(ins)
        nc.scalar.wait_ge(cdc.sem, init_consts)
        for _ in range(12):
            nc.scalar.nop(cycle_cnt=500)
        ins = nc.scalar.activation(apos[:], apos[:], AF.Exp)
        cS.inc(ins)
        nc.vector.wait_ge(cS.sem, cS.n)
        ins = nc.vector.tensor_scalar_mul(apos[:], apos[:], -1.0)
        cV.inc(ins)

        grid = _grid_consts()
        with (
            nc.sbuf_tensor("stat_r", [128, NSLAB * 48], BF16) as stat_r,
            nc.sbuf_tensor("aring", [128, PACK * 32], BF16) as aring,
            nc.sbuf_tensor("stage", [48, SLABGRP * 4 * PACK], BF16) as stage,
        ):
            ARING = 32
            ins = nc.gpsimd.dma_start(out=stat_r[:], in_=stat_h)
            cdc.dma(ins)
            stat_ready = cdc.n
            av = aconst_h.rearrange("s p c -> p s c")
            for s0 in range(0, NSLAB, 8):
                slot = s0 % ARING
                n_s = min(8, NSLAB - s0)
                if s0 >= ARING:
                    nc.sync.wait_ge(cRT.sem, s0 - ARING + 8)
                ins = nc.sync.dma_start(
                    out=aring[:, slot * PACK:(slot + n_s) * PACK].rearrange(
                        "p (s c) -> p s c", s=n_s),
                    in_=av[:, s0:s0 + n_s, :])
                cRA.dma(ins)
                for j in range(n_s):
                    pass
            nc.tensor.wait_ge(cdc.sem, stat_ready)
            nc.vector.wait_ge(cdc.sem, stat_ready)
            nc.scalar.wait_ge(cdc.sem, stat_ready)
            for _ in range(12):
                nc.tensor.nop(cycle_cnt=500)
            ev_eng = [nc.vector, nc.scalar]
            for g_ in range(NGRP):
                s_lo = g_ * SLABGRP
                s_hi = min(s_lo + SLABGRP, NSLAB)
                for s_ in range(s_lo, s_hi):
                    slot = s_ % ARING
                    nc.tensor.wait_ge(cRA.sem, 16 * (s_ // 8 + 1))
                    if s_ % 8 == 0:
                        for _ in range(8):
                            nc.tensor.nop(cycle_cnt=500)
                    if s_ >= 2:
                        nc.tensor.wait_ge(cRE.sem, s_ - 1)
                    ins = None
                    for q in range(4):
                        if 4 * s_ + q >= NPACK:
                            break
                        bk = banks[(s_ % 2) * 4 + q]
                        ins = nc.tensor.matmul(
                            bk[0:48, 0:PACK],
                            stat_r[32 * q:32 * q + ROWS, s_ * 48:(s_ + 1) * 48],
                            aring[32 * q:32 * q + ROWS, slot * PACK:(slot + 1) * PACK],
                            start=True, stop=True, tile_position=(32 * q, 0))
                    cRT.inc(ins)
                    eng = ev_eng[s_ % 2]
                    eng.wait_ge(cRT.sem, cRT.n)
                    if g_ >= 1:
                        eng.wait_ge(crs.sem, stage_free_mark)
                    ins = None
                    for q in range(4):
                        if 4 * s_ + q >= NPACK:
                            break
                        bk = banks[(s_ % 2) * 4 + q]
                        col0 = ((s_ - s_lo) * 4 + q) * PACK
                        if eng is nc.scalar:
                            ins = eng.copy(stage[:, col0:col0 + PACK], bk[0:48, 0:PACK])
                        else:
                            ins = eng.tensor_copy(stage[:, col0:col0 + PACK], bk[0:48, 0:PACK])
                    cRE.inc(ins)
                v_lo = s_lo * 4 * PACK
                v_hi = min(s_hi * 4 * PACK, PERIOD)
                glen = v_hi - v_lo
                nc.gpsimd.wait_ge(cRE.sem, cRE.n)
                for mat in range(3):
                    for po in range(15):
                        row = mat * 16 + po
                        ins = nc.gpsimd.dma_start(
                            out=wflat[mat][po * PERIOD + v_lo: po * PERIOD + v_hi].unsqueeze(0),
                            in_=stage[row:row + 1, 0:glen])
                        crs.dma(ins)
                if g_ == 0:
                    for mat in range(3):
                        row = mat * 16 + 15
                        ins = nc.gpsimd.dma_start(
                            out=wflat[mat][NSAMP - 1: NSAMP].unsqueeze(0),
                            in_=stage[row:row + 1, 0:1])
                        crs.dma(ins)
                stage_free_mark = crs.n
            nc.sync.wait_ge(crs.sem, crs.n)
            nc.tensor.wait_ge(cRE.sem, cRE.n)
            with (
                nc.sbuf_tensor("tpin", [128, 6 * 2048], BF16) as tpin,
                nc.sbuf_tensor("tpasm", [128, 2 * 512], BF16) as tpasm,
            ):
                cTP = Ctr(nc, st, "cTP")
                cTT = Ctr(nc, st, "cTT")
                cTEv = Ctr(nc, st, "cTEv")
                cTEs = Ctr(nc, st, "cTEs")
                nblk = 0
                specs = [(wg_h, wgT_h, INTER, D), (wu_h, wuT_h, INTER, D),
                         (wd_h, wdT_h, D, INTER)]
                asm_marks = {}
                ld_marks = {}
                for mat, (wsrc, wdst, R, C) in enumerate(specs):
                    for cs in range(C // 2048):
                        for oq in range(R // 512):
                            for b_ in range(4):
                                slot = nblk % 6
                                if nblk >= 6:
                                    nc.sync.wait_ge(cTT.sem, 64 * ((nblk - 6) // 4 + 1))
                                ins = nc.sync.dma_start(
                                    out=tpin[:, slot * 2048:(slot + 1) * 2048],
                                    in_=wsrc[oq * 512 + b_ * 128: oq * 512 + (b_ + 1) * 128,
                                             cs * 2048:(cs + 1) * 2048])
                                cTP.dma(ins)
                                ld_marks[nblk] = cTP.n
                                nblk += 1
                            base = nblk - 4
                            base_ev = [cTEv.n, cTEs.n]
                            nc.tensor.wait_ge(cTEv.sem, base_ev[0])
                            nc.tensor.wait_ge(cTEs.sem, base_ev[1])
                            for ct in range(16):
                                aslot = ct % 2
                                for b_ in range(4):
                                    k = ct * 4 + b_
                                    bk = banks[k % 4]
                                    nc.tensor.wait_ge(cTP.sem, ld_marks[base + b_])
                                    if ct == 0:
                                        for _ in range(16):
                                            nc.tensor.nop(cycle_cnt=500)
                                    if k >= 4:
                                        kprev = k - 4
                                        ctr = cTEs if kprev % 2 else cTEv
                                        nc.tensor.wait_ge(ctr.sem,
                                                          base_ev[kprev % 2] + kprev // 2 + 1)
                                    ins = nc.tensor.matmul(
                                        bk[0:128, 0:128],
                                        tpin[:, ((base + b_) % 6) * 2048 + ct * 128:
                                             ((base + b_) % 6) * 2048 + (ct + 1) * 128],
                                        identb[:], start=True, stop=True)
                                    cTT.inc(ins)
                                tt_all = cTT.n
                                for b_ in range(4):
                                    k = ct * 4 + b_
                                    bk = banks[k % 4]
                                    eng = nc.scalar if k % 2 else nc.vector
                                    eng.wait_ge(cTT.sem, tt_all)
                                    if ct >= 2 and b_ < 2:
                                        eng.wait_ge(crs.sem, asm_marks.get((nblk, ct - 2), 0))
                                    if ct < 2 and b_ < 2:
                                        eng.wait_ge(crs.sem, crs.n)
                                    if eng is nc.scalar:
                                        ins = eng.copy(
                                            tpasm[:, aslot * 512 + b_ * 128: aslot * 512 + (b_ + 1) * 128],
                                            bk[0:128, 0:128])
                                        cTEs.inc(ins)
                                    else:
                                        ins = eng.tensor_copy(
                                            tpasm[:, aslot * 512 + b_ * 128: aslot * 512 + (b_ + 1) * 128],
                                            bk[0:128, 0:128])
                                        cTEv.inc(ins)
                                nc.sync.wait_ge(cTEv.sem, cTEv.n)
                                nc.sync.wait_ge(cTEs.sem, cTEs.n)
                                ins = nc.sync.dma_start(
                                    out=wdst[cs * 2048 + ct * 128: cs * 2048 + (ct + 1) * 128,
                                             oq * 512:(oq + 1) * 512],
                                    in_=tpasm[:, aslot * 512:(aslot + 1) * 512])
                                crs.dma(ins)
                                asm_marks[(nblk, ct)] = crs.n
        recon_stores = crs.n

        xin = st.enter_context(nc.sbuf_tensor("xin_sb", [128, 2 * D], F32))
        out1 = st.enter_context(nc.sbuf_tensor("out1_sb", [128, NDT * TP], F32))
        xn_ring = st.enter_context(nc.sbuf_tensor("xn_ring", [128, 2 * T], F32))
        xnbf = st.enter_context(nc.sbuf_tensor("xnbf_sb", [128, NDT * T], BF16))
        negd = st.enter_context(nc.sbuf_tensor("negd_sb", [128, 2 * T], BF16))
        sigt = st.enter_context(nc.sbuf_tensor("sigt_sb", [128, T], F32))
        ball = st.enter_context(nc.sbuf_tensor("ball_sb", [128, NST * T], BF16))
        call = st.enter_context(nc.sbuf_tensor("call_sb", [128, NST * (T + 1)], BF16))
        hall = st.enter_context(nc.sbuf_tensor("hall_sb", [128, NST * (T + 1)], F32))
        hcb = st.enter_context(nc.sbuf_tensor("hc_sb", [128, 2 * 8 * (T + 1)], BF16))
        oinit = st.enter_context(nc.sbuf_tensor("oinit_sb", [128, 1], F32))
        xpbf = st.enter_context(nc.sbuf_tensor("xpbf_sb", [DTR, T], BF16))
        xpB = st.enter_context(nc.sbuf_tensor("xpB_sb", [16, T], BF16))
        xpC = st.enter_context(nc.sbuf_tensor("xpC_sb", [16, T], BF16))
        ubf = st.enter_context(nc.sbuf_tensor("ubf_sb", [128, 2 * T], BF16))
        abar = st.enter_context(nc.sbuf_tensor("abar_sb", [128, 4 * (T + 1)], F32))
        bxr = st.enter_context(nc.sbuf_tensor("bxr_sb", [128, 4 * (T + 1)], BF16))
        xbf_ring = st.enter_context(nc.sbuf_tensor("xbf_ring", [128, 4 * T], BF16))
        xsq_ring = st.enter_context(nc.sbuf_tensor("xsq_ring", [128, 4 * T], BF16))
        tmp_sb = st.enter_context(nc.sbuf_tensor("tmp_sb", [128, TP], F32))
        sg_ring = st.enter_context(nc.sbuf_tensor("sg_ring", [128, 2 * TP], BF16))
        ub_ring = st.enter_context(nc.sbuf_tensor("ub_ring", [128, 2 * TP], BF16))
        xn2 = st.enter_context(nc.sbuf_tensor("xn2_sb", [128, NDT * TP], BF16))
        inter_g = st.enter_context(nc.sbuf_tensor("inter_sb", [128, IGS * TP], BF16))
        gwr = st.enter_context(nc.sbuf_tensor("gw_ring", [128, 2 * NDT * 256], BF16,
                                              is_transpose_dest=True))
        uwr = st.enter_context(nc.sbuf_tensor("uw_ring", [128, 2 * NDT * 256], BF16,
                                              is_transpose_dest=True))
        dwr = st.enter_context(nc.sbuf_tensor("dw_ring", [128, IGS * 256], BF16,
                                              is_transpose_dest=True))
        ostage = st.enter_context(nc.sbuf_tensor("ostage_sb", [128, D // 2], F32))

        for eng in (nc.tensor, nc.vector, nc.scalar, nc.gpsimd, nc.sync):
            eng.wait_ge(crs.sem, recon_stores)
        nc.vector.memset(oinit[:], 1.0)
        bx3 = bxr[:].rearrange("p (s t) -> p s t", s=4)
        ca3 = call[:].rearrange("p (n t) -> p n t", n=NST)
        nc.vector.memset(bx3[:, :, 0], 0.0)
        ins = nc.vector.memset(ca3[:, :, 0], 0.0)
        cV.inc(ins)

        def ewait(eng, ctr, val):
            eng.wait_ge(ctr.sem, val)

        xload_done = [0] * NCH
        tposed_done = [0] * NCH

        def emit_xload(c):
            t0 = c * T
            if c >= 1:
                nc.sync.wait_ge(cT.sem, tposed_done[c - 1])
            for tt in range(2):
                ins = nc.sync.dma_start(out=xin[:, tt * D:(tt + 1) * D],
                                        in_=x_h[t0 + tt * 128: t0 + (tt + 1) * 128, :])
                cdx.dma(ins)
            xload_done[c] = cdx.n

        def emit_mamba(c):
            cpar = c % 2
            ocol = lambda dt, t=0: dt * TP + cpar * T + t
            sum_marks = [0] * NDT
            ewait(nc.tensor, cdx, xload_done[c])
            if c == 0:
                for _ in range(8):
                    nc.tensor.nop(cycle_cnt=500)
            ewait(nc.tensor, cS, cS.n)
            ewait(nc.tensor, cV, cV.n)
            ev_marks = [0] * (2 * NDT)
            stg_marks = [0] * NDT
            for dt in range(NDT):
                rs = dt % 4
                for tt in range(2):
                    k = dt * 2 + tt
                    bk = banks[k % 2]
                    if k >= 2:
                        ewait(nc.tensor, cS, ev_marks[k - 2])
                    ins = nc.tensor.transpose(
                        bk[:, 0:128],
                        xin[:, tt * D + dt * 128: tt * D + (dt + 1) * 128],
                        identf[:])
                    cT.inc(ins)
                    ewait(nc.scalar, cT, cT.n)
                    ins = nc.scalar.copy(
                        out1[:, ocol(dt, tt * 128): ocol(dt, tt * 128) + 128],
                        bk[:, 0:128])
                    cS.inc(ins)
                    ev_marks[k] = cS.n
                if dt >= 4:
                    ewait(nc.scalar, cT, sum_marks[dt - 4])
                ins = nc.scalar.copy(xbf_ring[:, rs * T:(rs + 1) * T],
                                     out1[:, ocol(dt): ocol(dt) + T])
                cS.inc(ins)
                ins = nc.scalar.activation(xsq_ring[:, rs * T:(rs + 1) * T],
                                           out1[:, ocol(dt): ocol(dt) + T], AF.Square)
                cS.inc(ins)
                stg_marks[dt] = cS.n
                ewait(nc.tensor, cS, stg_marks[dt])
                nc.tensor.matmul(banks[2][0:1, 0:T], onescol[:],
                                 xbf_ring[:, rs * T:(rs + 1) * T],
                                 start=(dt == 0), stop=(dt == NDT - 1))
                ins = nc.tensor.matmul(banks[3][0:1, 0:T], onescol[:],
                                       xsq_ring[:, rs * T:(rs + 1) * T],
                                       start=(dt == 0), stop=(dt == NDT - 1))
                cT.inc(ins)
                sum_marks[dt] = cT.n
            tposed_done[c] = cT.n
            sums_done = cT.n
            sv = stats[:, 0:T]
            sr = stats[:, T:2 * T]
            sc = stats[:, 2 * T:3 * T]
            ewait(nc.vector, cT, sums_done)
            nc.vector.tensor_scalar_mul(sv, banks[2][0:1, 0:T], 1.0 / D)
            nc.vector.tensor_scalar_mul(sr, banks[3][0:1, 0:T], 1.0 / D)
            nc.vector.tensor_tensor(sc, sv, sv, op=ALU.mult)
            nc.vector.tensor_tensor(sr, sr, sc, op=ALU.subtract)
            ins = nc.vector.tensor_scalar_add(sr, sr, LN_EPS)
            cV.inc(ins)
            ewait(nc.scalar, cV, cV.n)
            ins = nc.scalar.activation(sr, sr, AF.Sqrt)
            cS.inc(ins)
            ewait(nc.vector, cS, cS.n)
            nc.vector.reciprocal(sr, sr)
            nc.vector.tensor_tensor(sc, sv, sr, op=ALU.mult)
            ins = nc.vector.tensor_scalar_mul(sc, sc, -1.0)
            cV.inc(ins)
            stats_done = cV.n
            ewait(nc.tensor, cV, stats_done)
            nc.tensor.matmul(banks[4][:, 0:T], onesrowf[:], sr, start=True, stop=True)
            ins = nc.tensor.matmul(banks[5][:, 0:T], onesrowf[:], sc, start=True, stop=True)
            cT.inc(ins)
            bcast_done = cT.n
            ewait(nc.vector, cT, bcast_done)
            for dt in range(NDT):
                xr = dt % 2
                xnv = xn_ring[:, xr * T:(xr + 1) * T]
                if dt >= 2:
                    ewait(nc.vector, cS, xnbf_marks[dt - 2])
                nc.vector.tensor_tensor(xnv, out1[:, ocol(dt): ocol(dt) + T],
                                        banks[4][:, 0:T], op=ALU.mult)
                ins = nc.vector.tensor_tensor(xnv, xnv, banks[5][:, 0:T], op=ALU.add)
                cV.inc(ins)
                ins = nc.vector.scalar_tensor_tensor(
                    out1[:, ocol(dt): ocol(dt) + T], xnv, dcol[:, dt:dt + 1],
                    out1[:, ocol(dt): ocol(dt) + T], op0=ALU.mult, op1=ALU.add)
                cV.inc(ins)
                ewait(nc.scalar, cV, cV.n)
                ins = nc.scalar.copy(xnbf[:, dt * T:(dt + 1) * T], xnv)
                cS.inc(ins)
                if dt == 0:
                    xnbf_marks = [0] * NDT
                xnbf_marks[dt] = cS.n
                ewait(nc.tensor, cS, cS.n)
                c0 = dt * PXP
                nc.tensor.matmul(banks[2][0:DTR, 0:T], wxT[:, c0:c0 + DTR],
                                 xnbf[:, dt * T:(dt + 1) * T],
                                 start=(dt == 0), stop=(dt == NDT - 1))
                nc.tensor.matmul(banks[3][0:16, 0:T], wxT[:, c0 + DTR:c0 + DTR + 16],
                                 xnbf[:, dt * T:(dt + 1) * T],
                                 start=(dt == 0), stop=(dt == NDT - 1))
                ins = nc.tensor.matmul(banks[6][0:16, 0:T], wxT[:, c0 + DTR + 16:c0 + PXP],
                                       xnbf[:, dt * T:(dt + 1) * T],
                                       start=(dt == 0), stop=(dt == NDT - 1))
            cT.inc(ins)
            xp_done = cT.n
            ewait(nc.scalar, cT, xp_done)
            nc.scalar.copy(xpbf[:], banks[2][0:DTR, 0:T])
            nc.scalar.copy(xpB[:], banks[3][0:16, 0:T])
            ins = nc.scalar.copy(xpC[:], banks[6][0:16, 0:T])
            cS.inc(ins)
            xpbf_done = cS.n
            ewait(nc.tensor, cS, xpbf_done)
            ewait(nc.tensor, cV, cV.n)
            if c >= 1:
                ewait(nc.tensor, cVs, c * NDT * NST)
                ewait(nc.tensor, cG, 2 * c * NDT)
            bcev_marks = [0] * NST
            for n in range(NST):
                bk = banks[6 + n % 2]
                if n >= 2:
                    ewait(nc.tensor, cS, bcev_marks[n - 2])
                nc.tensor.matmul(bk[:, 0:T], possel[:, n * 128:(n + 1) * 128],
                                 xpB[:], start=True, stop=True)
                ins = nc.tensor.matmul(bk[:, T:2 * T], possel[:, n * 128:(n + 1) * 128],
                                       xpC[:], start=True, stop=True)
                cT.inc(ins)
                ewait(nc.scalar, cT, cT.n)
                nc.scalar.copy(ball[:, n * T:(n + 1) * T], bk[:, 0:T])
                ins = nc.scalar.copy(call[:, n * (T + 1) + 1:(n + 1) * (T + 1)],
                                     bk[:, T:2 * T])
                cS.inc(ins)
                bcev_marks[n] = cS.n
            bc_done = cS.n
            TP1 = T + 1
            for dt in range(NDT):
                ndr = dt % 2
                dbk = banks[2 + dt % 2]
                if dt >= 2:
                    ewait(nc.tensor, cS, sig_marks[dt - 2])
                ins = nc.tensor.matmul(dbk[0:128, 0:T], wdtT[:, dt * 128:(dt + 1) * 128],
                                       xpbf[0:DTR, :], start=True, stop=True)
                cT.inc(ins)
                ewait(nc.scalar, cT, cT.n)
                ewait(nc.scalar, cVs, cVs.n)
                ewait(nc.scalar, cV, cV.n)
                nc.scalar.activation(sigt[:], dbk[0:128, 0:T], AF.Relu,
                                     bias=bdt[:, dt:dt + 1])
                nc.scalar.activation(xn_ring[:, 0:T], dbk[0:128, 0:T], AF.Abs,
                                     bias=bdt[:, dt:dt + 1])
                nc.scalar.activation(xn_ring[:, T:2 * T], xn_ring[:, 0:T], AF.Exp,
                                     scale=-1.0)
                ins = nc.scalar.activation(xn_ring[:, 0:T], xn_ring[:, T:2 * T], AF.Ln,
                                           bias=oinit[:, 0:1])
                cS.inc(ins)
                if dt == 0:
                    sig_marks = [0] * NDT
                sig_marks[dt] = cS.n
                ewait(nc.vector, cS, sig_marks[dt])
                ins = nc.vector.scalar_tensor_tensor(
                    negd[:, ndr * T:(ndr + 1) * T], sigt[:], oinit[:, 0:1], xn_ring[:, 0:T],
                    op0=ALU.mult, op1=ALU.add)
                cV.inc(ins)
                posd_mark = cV.n
                ewait(nc.scalar, cV, posd_mark)
                if c > 0 or dt > 0:
                    ewait(nc.vector, cG, cG.n)
                ins = nc.vector.tensor_tensor(ubf[:, ndr * T:(ndr + 1) * T],
                                              negd[:, ndr * T:(ndr + 1) * T],
                                              xnbf[:, dt * T:(dt + 1) * T], op=ALU.mult)
                cV.inc(ins)
                ewait(nc.vector, cS, bc_done)

                scan_marks = [0] * NST

                def emit_bx(k):
                    ar = k % 4
                    nc.vector.tensor_tensor(
                        bxr[:, ar * TP1 + 1:(ar + 1) * TP1],
                        ubf[:, ndr * T:(ndr + 1) * T],
                        ball[:, k * T:(k + 1) * T], op=ALU.mult)

                for n in range(NST):
                    ar = n % 4
                    g, r = divmod(n, 4)
                    if r == 0:
                        if g > 0:
                            ewait(nc.scalar, cVs, scan_marks[4 * g - 1])
                        a3 = abar[:].rearrange("p (s t) -> p s t", s=4)
                        nc.scalar.copy(a3[:, :, 0],
                                       state[:, dt * NST + 4 * g: dt * NST + 4 * g + 4])
                    av = abar[:, ar * TP1:(ar + 1) * TP1]
                    if n >= 4:
                        ewait(nc.scalar, cVs, scan_marks[n - 4])
                    ins = nc.scalar.activation(
                        av[:, 1:TP1], negd[:, ndr * T:(ndr + 1) * T], AF.Exp,
                        scale=apos[:, dt * NST + n: dt * NST + n + 1])
                    cS.inc(ins)
                    if n == 0:
                        emit_bx(0)
                        emit_bx(1)
                        emit_bx(2)
                    elif n + 2 < NST:
                        emit_bx(n + 2)
                    ewait(nc.vector, cS, cS.n)
                    ins = nc.vector.tensor_tensor_scan(
                        hall[:, n * TP1:(n + 1) * TP1], av,
                        bxr[:, ar * TP1:(ar + 1) * TP1],
                        oinit[:, 0:1], op0=ALU.mult, op1=ALU.add)
                    cVs.inc(ins)
                    scan_marks[n] = cVs.n
                    if n == 7 or n == 15:
                        half = n // 8
                        ewait(nc.gpsimd, cVs, cVs.n)
                        if c > 0 or dt > 0 or half > 0:
                            ewait(nc.gpsimd, cT,
                                  yhalf_marks.get((dt + NDT * c) * 2 + half - 2, 0))
                        ins3 = nc.gpsimd.tensor_tensor(
                            hcb[:, half * 8 * TP1:(half + 1) * 8 * TP1],
                            hall[:, half * 8 * TP1:(half + 1) * 8 * TP1],
                            call[:, half * 8 * TP1:(half + 1) * 8 * TP1], op=ALU.mult)
                        cG.inc(ins3)
                        ybk = banks[6 + dt % 2]
                        ewait(nc.tensor, cG, cG.n)
                        if half == 0 and dt >= 2:
                            ewait(nc.tensor, cV, yev_marks[dt - 2])
                        ins4 = None
                        for nn in range(8):
                            ins4 = nc.tensor.matmul(
                                ybk[:, 0:T], identb[:],
                                hcb[:, half * 8 * TP1 + nn * TP1 + 1:
                                    half * 8 * TP1 + (nn + 1) * TP1],
                                start=(half == 0 and nn == 0),
                                stop=(half == 1 and nn == 7))
                        cT.inc(ins4)
                        yhalf_marks[(dt + NDT * c) * 2 + half] = cT.n
                ewait(nc.scalar, cVs, scan_marks[NST - 1])
                h3 = hall[:].rearrange("p (n t) -> p n t", n=NST)
                ins = nc.scalar.copy(state[:, dt * NST:(dt + 1) * NST], h3[:, :, T])
                cS.inc(ins)
                ewait(nc.vector, cT, cT.n)
                ins = nc.vector.tensor_tensor(out1[:, ocol(dt): ocol(dt) + T],
                                              out1[:, ocol(dt): ocol(dt) + T],
                                              banks[6 + dt % 2][:, 0:T], op=ALU.add)
                cV.inc(ins)
                if dt == 0:
                    yev_marks = [0] * NDT
                yev_marks[dt] = cV.n

        yhalf_marks = {}

        gw_marks = {}
        dw_marks = {}

        def emit_wload_gu(p, iq):
            slot = iq % 2
            gkey = p * 32 + iq
            prev = gw_marks.get(gkey - 2)
            if prev:
                nc.sync.wait_ge(cT.sem, prev)
            gv = wgT_h.rearrange("(t p) c -> p t c", p=128)[:, :, iq * 256:(iq + 1) * 256]
            uv = wuT_h.rearrange("(t p) c -> p t c", p=128)[:, :, iq * 256:(iq + 1) * 256]
            do = gwr[:, slot * NDT * 256:(slot + 1) * NDT * 256].rearrange(
                "p (t c) -> p t c", t=NDT)
            uo = uwr[:, slot * NDT * 256:(slot + 1) * NDT * 256].rearrange(
                "p (t c) -> p t c", t=NDT)
            ins = nc.sync.dma_start(out=do, in_=gv)
            cdw.dma(ins)
            ins = nc.sync.dma_start(out=uo, in_=uv)
            cdw.dma(ins)
            return cdw.n

        def emit_wload_down(p, ig, hp):
            idx = p * NIG * 8 + ig * 8 + hp
            prev = dw_marks.get(idx - 1)
            if prev:
                nc.sync.wait_ge(cT.sem, prev)
            dv = wdT_h.rearrange("(t p) c -> p t c", p=128)[
                :, ig * IGS:(ig + 1) * IGS, hp * 256:(hp + 1) * 256]
            do = dwr[:].rearrange("p (t c) -> p t c", t=IGS)
            ins = nc.sync.dma_start(out=do, in_=dv)
            cdw.dma(ins)
            return cdw.n

        def emit_ffn(p):
            ln2_marks = [0] * NDT
            for dt in range(NDT):
                sl = dt % 2
                ocol0 = dt * TP
                if dt >= 2:
                    ewait(nc.scalar, cT, ln2_marks[dt - 2])
                ins = nc.scalar.copy(sg_ring[:, sl * TP: sl * TP + TP],
                                     out1[:, ocol0:ocol0 + TP])
                cS.inc(ins)
                ins = nc.scalar.activation(ub_ring[:, sl * TP: sl * TP + TP],
                                           out1[:, ocol0:ocol0 + TP], AF.Square)
                cS.inc(ins)
                ewait(nc.tensor, cS, cS.n)
                nc.tensor.matmul(banks[2][0:1, 0:TP], onescol[:],
                                 sg_ring[:, sl * TP: sl * TP + TP],
                                 start=(dt == 0), stop=(dt == NDT - 1))
                ins = nc.tensor.matmul(banks[3][0:1, 0:TP], onescol[:],
                                       ub_ring[:, sl * TP: sl * TP + TP],
                                       start=(dt == 0), stop=(dt == NDT - 1))
                cT.inc(ins)
                ln2_marks[dt] = cT.n
            sums_done = cT.n
            sv = stats[:, 0:TP]
            sr = stats[:, TP:2 * TP]
            sc = stats[:, 2 * TP:3 * TP]
            ewait(nc.vector, cT, sums_done)
            nc.vector.tensor_scalar_mul(sv, banks[2][0:1, 0:TP], 1.0 / D)
            nc.vector.tensor_scalar_mul(sr, banks[3][0:1, 0:TP], 1.0 / D)
            nc.vector.tensor_tensor(sc, sv, sv, op=ALU.mult)
            nc.vector.tensor_tensor(sr, sr, sc, op=ALU.subtract)
            ins = nc.vector.tensor_scalar_add(sr, sr, LN_EPS)
            cV.inc(ins)
            ewait(nc.scalar, cV, cV.n)
            ins = nc.scalar.activation(sr, sr, AF.Sqrt)
            cS.inc(ins)
            ewait(nc.vector, cS, cS.n)
            nc.vector.reciprocal(sr, sr)
            nc.vector.tensor_tensor(sc, sv, sr, op=ALU.mult)
            ins = nc.vector.tensor_scalar_mul(sc, sc, -1.0)
            cV.inc(ins)
            stats_done = cV.n
            ewait(nc.tensor, cV, stats_done)
            nc.tensor.matmul(banks[4][:, 0:TP], onesrowf[:], sr, start=True, stop=True)
            ins = nc.tensor.matmul(banks[5][:, 0:TP], onesrowf[:], sc, start=True, stop=True)
            cT.inc(ins)
            bcast_done = cT.n
            ewait(nc.vector, cT, bcast_done)
            for dt in range(NDT):
                nc.vector.tensor_tensor(tmp_sb[:], out1[:, dt * TP:(dt + 1) * TP],
                                        banks[4][:, 0:TP], op=ALU.mult)
                ins = nc.vector.tensor_tensor(xn2[:, dt * TP:(dt + 1) * TP], tmp_sb[:],
                                              banks[5][:, 0:TP], op=ALU.add)
                cV.inc(ins)
            xn2_done = cV.n

            gw_dones = {0: emit_wload_gu(p, 0)}
            for ig in range(NIG):
                for ii in range(IGS):
                    i = ig * IGS + ii
                    iq, ip = divmod(i, 2)
                    slot = iq % 2
                    if ip == 0:
                        gw_done = gw_dones[iq]
                        if iq + 1 < NIT // 2:
                            gw_dones[iq + 1] = emit_wload_gu(p, iq + 1)
                    gbk = banks[i % 2]
                    ubk = banks[2 + i % 2]
                    ewait(nc.tensor, cdw, gw_done)
                    if i >= 2:
                        ewait(nc.tensor, cS, silu_marks[i - 2])
                    ewait(nc.tensor, cV, xn2_done)
                    for dt in range(NDT):
                        nc.tensor.matmul(gbk[:, 0:TP],
                                         gwr[:, slot * NDT * 256 + dt * 256 + ip * 128:
                                             slot * NDT * 256 + dt * 256 + ip * 128 + 128],
                                         xn2[:, dt * TP:(dt + 1) * TP],
                                         start=(dt == 0), stop=(dt == NDT - 1))
                    for dt in range(NDT):
                        ins = nc.tensor.matmul(ubk[:, 0:TP],
                                               uwr[:, slot * NDT * 256 + dt * 256 + ip * 128:
                                                   slot * NDT * 256 + dt * 256 + ip * 128 + 128],
                                               xn2[:, dt * TP:(dt + 1) * TP],
                                               start=(dt == 0), stop=(dt == NDT - 1))
                    cT.inc(ins)
                    if ip == 1:
                        gw_marks[p * 32 + iq] = cT.n
                    mm_done = cT.n
                    ewait(nc.scalar, cT, mm_done)
                    nc.scalar.activation(sg_ring[:, (i % 2) * TP:(i % 2) * TP + TP],
                                         gbk[:, 0:TP], AF.Silu)
                    ins = nc.scalar.copy(ub_ring[:, (i % 2) * TP:(i % 2) * TP + TP],
                                         ubk[:, 0:TP])
                    cS.inc(ins)
                    if i == 0:
                        silu_marks = [0] * NIT
                    silu_marks[i] = cS.n
                    ewait(nc.vector, cS, silu_marks[i])
                    ins = nc.vector.tensor_tensor(inter_g[:, ii * TP:(ii + 1) * TP],
                                                  sg_ring[:, (i % 2) * TP:(i % 2) * TP + TP],
                                                  ub_ring[:, (i % 2) * TP:(i % 2) * TP + TP],
                                                  op=ALU.mult)
                    cV.inc(ins)
                inter_done = cV.n
                for hp_ in range(8):
                    dw_done = emit_wload_down(p, ig, hp_)
                    for hh in range(2):
                        h = hp_ * 2 + hh
                        dbk = banks[6 + h % 2]
                        ewait(nc.tensor, cdw, dw_done)
                        if hh == 0:
                            for _ in range(3):
                                nc.tensor.nop(cycle_cnt=500)
                        if h == 0:
                            dnev_marks = [0] * 16
                            ewait(nc.tensor, cV, inter_done)
                        elif h >= 2:
                            ewait(nc.tensor, cV, dnev_marks[h - 2])
                        ins = None
                        for ii in range(IGS):
                            ins = nc.tensor.matmul(
                                dbk[:, 0:TP],
                                dwr[:, ii * 256 + hh * 128: ii * 256 + hh * 128 + 128],
                                inter_g[:, ii * TP:(ii + 1) * TP],
                                start=(ii == 0), stop=(ii == IGS - 1))
                        cT.inc(ins)
                        if hh == 1:
                            dw_marks[p * NIG * 8 + ig * 8 + hp_] = cT.n
                        ewait(nc.vector, cT, cT.n)
                        ins = nc.vector.tensor_tensor(out1[:, h * TP:(h + 1) * TP],
                                                      out1[:, h * TP:(h + 1) * TP],
                                                      dbk[:, 0:TP], op=ALU.add)
                        cV.inc(ins)
                        dnev_marks[h] = cV.n
            final_done = cV.n
            t0 = p * TP
            ewait(nc.tensor, cV, final_done)
            fin_marks = [0] * (4 * NDT)
            for tt in range(4):
                for dh in range(2):
                    if tt + dh >= 1:
                        ewait(nc.scalar, cds, cds.n)
                    for dt2 in range(NDT // 2):
                        dt = dh * (NDT // 2) + dt2
                        bk = banks[dt % 2]
                        kk = tt * NDT + dh * (NDT // 2) + dt2
                        if kk >= 2:
                            ewait(nc.tensor, cS, fin_marks[kk - 2])
                        ins = nc.tensor.transpose(
                            bk[:, 0:128],
                            out1[:, dt * TP + tt * 128: dt * TP + tt * 128 + 128],
                            identf[:])
                        cT.inc(ins)
                        ewait(nc.scalar, cT, cT.n)
                        ins = nc.scalar.copy(ostage[:, dt2 * 128:(dt2 + 1) * 128],
                                             bk[:, 0:128])
                        cS.inc(ins)
                        fin_marks[kk] = cS.n
                    ewait(nc.sync, cS, cS.n)
                    ins = nc.sync.dma_start(
                        out=out_h[t0 + tt * 128: t0 + (tt + 1) * 128,
                                  dh * (D // 2):(dh + 1) * (D // 2)],
                        in_=ostage[:])
                    cds.dma(ins)

        emit_xload(0)
        for p in range(NPASS):
            for half in range(2):
                c = 2 * p + half
                emit_mamba(c)
                if c + 1 < NCH:
                    emit_xload(c + 1)
            emit_ffn(p)

        nc.sync.wait_ge(cds.sem, cds.n)
        nc.gpsimd.wait_ge(cds.sem, cds.n)
    return nc


def _prep_inputs(inputs):
    x = np.ascontiguousarray(inputs["L0_attn_x"])
    A_log = inputs["L0_attn_A_log"]
    Dv = inputs["L0_attn_D"]
    W_x = inputs["L0_attn_W_x"]
    W_dt = inputs["L0_attn_W_dt"]
    b_dt = inputs["L0_attn_b_dt"]
    grid = _grid_consts()
    small = _layout_consts()
    stat = _stat_host(inputs["L0_ffn_gate_cp"], inputs["L0_ffn_up_cp"],
                      inputs["L0_ffn_down_cp"], grid["i0_first"])
    wxT = np.ascontiguousarray(
        W_x.T.reshape(NDT, 128, PXP).transpose(1, 0, 2).reshape(128, NDT * PXP))
    wdtT = np.ascontiguousarray(W_dt.T)
    bdt = np.ascontiguousarray(b_dt.reshape(NDT, 128).T)
    dcol = np.ascontiguousarray(Dv.reshape(NDT, 128).T)
    alog = np.ascontiguousarray(
        A_log.reshape(NDT, 128, NST).transpose(1, 0, 2).reshape(128, NDT * NST))
    shared = dict(
        aconst=grid["A"], stat=stat, wxT=wxT, wdtT=wdtT, bdt=bdt, dcol=dcol,
        alog=alog, identf=small["ident_f32"], identb=small["ident_bf"],
        onescol=small["ones_col_bf"], onesrowf=small["ones_row_f32"],
        onesrowb=small["ones_row_bf"], negonesrowb=small["negones_row_bf"],
        possel=small["possel"], negsel=small["negsel"],
    )
    in_maps = []
    for b in range(B):
        m = dict(shared)
        m["x"] = np.ascontiguousarray(x[b])
        in_maps.append(m)
    return in_maps


def kernel(**inputs):
    in_maps = _prep_inputs(inputs)
    if "nc" not in _NC_CACHE:
        _NC_CACHE["nc"] = build_program()
    nc = _NC_CACHE["nc"]
    res = run_bass_kernel_spmd(nc, in_maps, core_ids=list(range(B)))
    out = np.stack([r["out"] for r in res.results], axis=0)
    return out.astype(np.float32)
